# version 01
# baseline (speedup 1.0000x reference)
import numpy as np
import ml_dtypes
from contextlib import ExitStack
import concourse.bass as bass
import concourse.mybir as mybir
from concourse.bass_utils import run_bass_kernel_spmd

F32 = mybir.dt.float32
BF16 = mybir.dt.bfloat16
I32 = mybir.dt.int32
AF = mybir.ActivationFunctionType
ALU = mybir.AluOpType
BF = ml_dtypes.bfloat16

B, S, D = 2, 4096, 2048
HC = 512
NTS = 8
TOK = 1024
NDT = D // 128
EPS = 1e-5
ROLES = ("sync", "tensor", "vector", "scalar", "gpsimd")
GROUPS = [[0, 1, 2, 3], [4, 5, 6, 7]]


def hc_owner(h):
    j = min(h, 7 - h)
    return j, (0 if h <= 3 else 1)


class P:

    def __init__(self, role, q, sems):
        self.role = role
        self.q = q
        self.sems = sems
        self.cnt = {k: 0 for k in sems}

    def inc(self, name, n=1):
        self.cnt[name] += n
        return self.sems[name]

    def on(self, role):
        return self.role == role

    def wait(self, role, name, val=None):
        if self.role == role:
            v = self.cnt[name] if val is None else val
            if v > 0:
                self.q.wait_ge(self.sems[name], v)

    def fin(self, role, name, n=1):
        sem = self.inc(name, n)
        if self.on(role):
            self.q.drain().then_inc(sem, n)
        return sem



def emit_ln_one(p, t, src, dst, so, ln_sem, pre_wait=None):
    if pre_wait is not None:
        pre_wait(p)
    if p.on("vector"):
        for c in range(4):
            p.q.bn_stats(t["stats"][:, 6 * c:6 * c + 6], src[:, 512 * c:512 * (c + 1)])
            p.q.drain()
        p.q.bn_aggr(t["aggr"][:, 2 * so:2 * so + 2], t["stats"][:, 0:24])
        p.q.drain()
        p.q.tensor_scalar(out=t["veps"][:, so:so + 1], in0=t["aggr"][:, 2 * so + 1:2 * so + 2],
                          scalar1=EPS, scalar2=None, op0=ALU.add)
        p.q.drain()
        p.q.reciprocal(t["recv"][:, so:so + 1], t["veps"][:, so:so + 1])
    p.fin("vector", "lnv")
    p.wait("scalar", "lnv")
    if p.on("scalar"):
        p.q.activation(out=t["rstd"][:, so:so + 1], in_=t["recv"][:, so:so + 1],
                       func=AF.Sqrt)
        p.q.mul(t["negm"][:, so:so + 1], t["aggr"][:, 2 * so:2 * so + 1], -1.0)
    p.fin("scalar", "lns")
    p.wait("vector", "lns")
    if p.on("vector"):
        p.q.tensor_scalar(out=dst, in0=src,
                          scalar1=t["negm"][:, so:so + 1], scalar2=t["rstd"][:, so:so + 1],
                          op0=ALU.add, op1=ALU.mult)
    p.fin("vector", ln_sem)


def emit_transpose_2048(p, t, ts, src, dstT, tp_snap):
    dstT3 = dstT.rearrange("p (dt tok) -> p dt tok", dt=NDT)
    for g in range(2):
        gi = 2 * ts + g
        pb = t["ptp"][gi % 2]
        sm = p.inc("tp")
        if p.on("tensor"):
            if gi >= 2:
                p.q.wait_ge(p.sems["tpe"], gi - 1)
            for k in range(8):
                dt = g * 8 + k
                ins = p.q.transpose(pb[:, k * 128:(k + 1) * 128],
                                    src[:, dt * 128:(dt + 1) * 128], t["ident"][:, :])
                if k == 7:
                    ins.then_inc(sm, 1)
        p.wait("vector", "tp")
        if p.on("vector"):
            dst = dstT3[:, g * 8:(g + 1) * 8, ts * 128:ts * 128 + 128]
            srcp = pb.rearrange("p (k c) -> p k c", k=8)
            p.q.tensor_copy(dst, srcp)
        p.fin("vector", "tpe")
    tp_snap.append((p.cnt["tp"], p.cnt["tpe"]))



def build_l0_attn():
    nc = bass.Bass("TRN2", target_bir_lowering=False)
    x_in = nc.dram_tensor("x_hc", [TOK, D], F32, kind="ExternalInput").ap()
    wq = nc.dram_tensor("wqT", [D, D], F32, kind="ExternalInput").ap()
    wk = nc.dram_tensor("wkT", [D, D], F32, kind="ExternalInput").ap()
    wv = nc.dram_tensor("wvT", [D, D], F32, kind="ExternalInput").ap()
    wo = nc.dram_tensor("woT", [D, D], F32, kind="ExternalInput").ap()
    ident_in = nc.dram_tensor("ident_hbm", [128, 128], BF16, kind="ExternalInput").ap()
    cmask_in = nc.dram_tensor("cmask_hbm", [128, 16], F32, kind="ExternalInput").ap()
    x_out = nc.dram_tensor("x1_hc", [TOK, D], F32, kind="ExternalOutput").ap()
    cin = nc.dram_tensor("cc_in", [128, 32], F32)
    cout = nc.dram_tensor("cc_out", [512, 32], F32)

    denom = float(np.sqrt(np.float32(D)) / S)

    with ExitStack() as est:
        t = {}
        def sb(name, shape, dt):
            t[name] = est.enter_context(nc.sbuf_tensor(name, shape, dt))
        def ps(name, shape, dt):
            t[name] = est.enter_context(nc.psum_tensor(name, shape, dt))

        sb("xbuf", [128, 2 * 2048], F32)
        sb("lnx", [128, 2 * 2048], BF16)
        sb("lnxT", [128, NDT * TOK], BF16)
        sb("qT", [128, NDT * TOK], BF16)
        sb("kv", [128, NDT * TOK], BF16)
        sb("ysb", [128, NTS * 2048], BF16)
        sb("wbuf", [128, 4 * NDT * 128], BF16)
        sb("woc", [128, 2 * NDT * 512], BF16)
        sb("ident", [128, 128], BF16)
        sb("cmaskb", [128, 16], F32)
        sb("stats", [128, 24], F32)
        sb("aggr", [128, 32], F32)
        sb("veps", [128, 16], F32)
        sb("recv", [128, 16], F32)
        sb("negm", [128, 16], F32)
        sb("rstd", [128, 16], F32)
        sb("sown", [128, 32], F32)
        sb("sgat", [128, 128], F32)
        sb("carry", [128, 2 * NDT], F32)
        sb("carry2", [128, 2 * NDT], F32)

        for i in range(4):
            ps(f"pp{i}", [128, 512], F32)
        t["pp"] = [t[f"pp{i}"] for i in range(4)]
        ps("ptp0", [128, 1024], BF16)
        ps("ptp1", [128, 1024], BF16)
        t["ptp"] = [t["ptp0"], t["ptp1"]]

        sem_names = ["xld", "wld", "wcons", "lnv", "lns", "ln1", "tp", "tpe",
                     "mm", "ev", "ccl", "cc", "cgat", "carr", "vh",
                     "yld", "ymm", "yev", "lny", "res", "sto"]
        sems = {n: est.enter_context(nc.semaphore(n)) for n in sem_names}
        block = est.enter_context(nc.Block())

        def program(p):
            lnxT = t["lnxT"]
            wbuf4 = t["wbuf"].rearrange("p (s dt c) -> p s dt c", s=4, dt=NDT)
            lnxT3 = lnxT.rearrange("p (dt tok) -> p dt tok", dt=NDT)
            kv3 = t["kv"].rearrange("p (dt tok) -> p dt tok", dt=NDT)

            s0 = p.inc("xld", 32)
            if p.on("sync"):
                p.q.dma_start(out=t["ident"][:, :], in_=ident_in[:, :]).then_inc(s0, 16)
                p.q.dma_start(out=t["cmaskb"][:, :], in_=cmask_in[:, :]).then_inc(s0, 16)
            prelude_xld = p.cnt["xld"]
            p.wait("tensor", "xld", prelude_xld)

            xld_snap, ln1_snap, tp_snap = [], [], []
            for ts in range(NTS):
                sx = p.inc("xld", 16)
                xld_snap.append(p.cnt["xld"])
                if p.on("sync"):
                    if ts >= 2:
                        p.q.wait_ge(p.sems["ln1"], ln1_snap[ts - 2])
                    p.q.dma_start(out=t["xbuf"][:, (ts % 2) * 2048:(ts % 2 + 1) * 2048],
                                  in_=x_in[ts * 128:(ts + 1) * 128, :]).then_inc(sx, 16)

                def prew(p2, ts=ts):
                    p2.wait("vector", "xld", xld_snap[ts])
                    if ts >= 2:
                        p2.wait("vector", "tp", tp_snap[ts - 2][0])

                emit_ln_one(p, t,
                            t["xbuf"][:, (ts % 2) * 2048:(ts % 2 + 1) * 2048],
                            t["lnx"][:, (ts % 2) * 2048:(ts % 2 + 1) * 2048],
                            ts, "ln1", pre_wait=prew)
                ln1_snap.append(p.cnt["ln1"])
                p.wait("tensor", "ln1", ln1_snap[ts])
                emit_transpose_2048(p, t, ts,
                                    t["lnx"][:, (ts % 2) * 2048:(ts % 2 + 1) * 2048],
                                    lnxT, tp_snap)

            tpe_total = p.cnt["tpe"]

            RING = 4
            wlist = [wq, wk, wv]
            wld_snap = []
            for ci in range(48):
                et, wi = divmod(ci, 3)
                sw = p.inc("wld", 16)
                wld_snap.append(p.cnt["wld"])
                if p.on("gpsimd"):
                    if ci >= RING:
                        p.q.wait_ge(p.sems["mm"], 2 * (ci - RING + 1))
                    src = wlist[wi].rearrange("(dt pp) e -> pp dt e", pp=128)[:, :, et * 128:(et + 1) * 128]
                    p.q.dma_start(out=wbuf4[:, ci % RING], in_=src).then_inc(sw, 16)

            ev_k_snap = {}
            for ci in range(48):
                et, wi = divmod(ci, 3)
                for th in range(2):
                    pr = 2 * ci + th
                    pb = t["pp"][pr % 4]
                    col = et * TOK + th * 512
                    sm = p.inc("mm")
                    if p.on("tensor"):
                        p.q.wait_ge(p.sems["wld"], wld_snap[ci])
                        if pr == 0:
                            p.q.wait_ge(p.sems["tpe"], tpe_total)
                        if pr >= 4:
                            p.q.wait_ge(p.sems["ev"], pr - 3)
                        for dt in range(NDT):
                            ins = p.q.matmul(pb[:, 0:512], wbuf4[:, ci % RING, dt],
                                             lnxT3[:, dt, th * 512:(th + 1) * 512],
                                             start=(dt == 0), stop=(dt == NDT - 1))
                            if dt == NDT - 1:
                                ins.then_inc(sm, 1)
                    if wi == 0:
                        p.wait("scalar", "mm")
                        if p.on("scalar"):
                            p.q.copy(t["qT"][:, col:col + 512], pb[:, 0:512])
                        p.fin("scalar", "ev")
                    elif wi == 1:
                        ev_k_snap[(et, th)] = p.cnt["ev"] + 1
                        p.wait("scalar", "mm")
                        if p.on("scalar"):
                            p.q.copy(t["kv"][:, col:col + 512], pb[:, 0:512])
                        p.fin("scalar", "ev")
                    else:
                        p.wait("vector", "mm")
                        p.wait("vector", "ev", ev_k_snap[(et, th)])
                        if p.on("vector"):
                            p.q.tensor_tensor(out=t["kv"][:, col:col + 512],
                                              in0=t["kv"][:, col:col + 512],
                                              in1=pb[:, 0:512], op=ALU.mult)
                            p.q.drain()
                            p.q.tensor_reduce(out=t["sown"][:, th * 16 + et:th * 16 + et + 1],
                                              in_=t["kv"][:, col:col + 512],
                                              axis=mybir.AxisListType.X, op=ALU.add)
                        p.fin("vector", "ev")


            ev_total = p.cnt["ev"]

            scl = p.inc("ccl", 16)
            if p.on("gpsimd"):
                p.q.wait_ge(p.sems["ev"], ev_total)
                p.q.dma_start(out=cin[:, :], in_=t["sown"][:, :]).then_inc(scl, 16)
            scc = p.inc("cc")
            if p.on("gpsimd"):
                p.q.wait_ge(p.sems["ccl"], p.cnt["ccl"])
                p.q.collective_compute(
                    "AllGather", ALU.bypass, replica_groups=GROUPS,
                    ins=[cin.ap().opt()], outs=[cout.ap().opt()]).then_inc(scc, 1)
            sgt = p.inc("cgat", 64)
            if p.on("gpsimd"):
                p.q.wait_ge(p.sems["cc"], p.cnt["cc"])
                for r in range(4):
                    p.q.dma_start(out=t["sgat"][:, r * 32:(r + 1) * 32],
                                  in_=cout[r * 128:(r + 1) * 128, :]).then_inc(sgt, 16)

            p.wait("vector", "cgat")
            p.wait("vector", "xld", prelude_xld)
            if p.on("vector"):
                p.q.memset(t["carry"][:, :], 0.0)
                p.q.drain()
                for side in range(2):
                    for hp in range(8):
                        j2, s2 = hc_owner(hp)
                        p.q.scalar_tensor_tensor(
                            out=t["carry"][:, side * NDT:(side + 1) * NDT],
                            in0=t["sgat"][:, j2 * 32 + s2 * 16:j2 * 32 + s2 * 16 + 16],
                            scalar=t["cmaskb"][:, side * 8 + hp:side * 8 + hp + 1],
                            in1=t["carry"][:, side * NDT:(side + 1) * NDT],
                            op0=ALU.mult, op1=ALU.add)
                        p.q.drain()
            p.fin("vector", "carr")
            p.wait("scalar", "carr")
            if p.on("scalar"):
                p.q.copy(t["carry2"][:, :], t["carry"][:, :])
            p.fin("scalar", "carr")
            p.wait("vector", "carr")

            for et in range(NDT):
                for th in range(2):
                    col = et * TOK + th * 512
                    if p.on("vector"):
                        p.q.tensor_tensor_scan(
                            out=lnxT[:, col:col + 512], data0=t["kv"][:, col:col + 512],
                            data1=t["kv"][:, col:col + 512],
                            initial=t["carry2"][:, th * NDT + et:th * NDT + et + 1],
                            op0=ALU.add, op1=ALU.bypass)
                        p.q.drain()
                        p.q.scalar_tensor_tensor(
                            out=t["kv"][:, col:col + 512], in0=lnxT[:, col:col + 512],
                            scalar=denom, in1=t["qT"][:, col:col + 512],
                            op0=ALU.mult, op1=ALU.mult)
                        p.q.drain()
            p.fin("vector", "vh")
            vh_total = p.cnt["vh"]

            yld_snap = []
            ymm_snap = []
            for fc in range(4):
                sy = p.inc("yld", 16 * NDT)
                yld_snap.append(p.cnt["yld"])
                if p.on("gpsimd"):
                    if fc >= 2:
                        p.q.wait_ge(p.sems["ymm"], ymm_snap[(fc - 2) * NTS + NTS - 1])
                    for et in range(NDT):
                        base = (fc % 2) * NDT * 512 + et * 512
                        p.q.dma_start(out=t["woc"][:, base:base + 512],
                                      in_=wo[et * 128:(et + 1) * 128, fc * 512:(fc + 1) * 512]).then_inc(sy, 16)
                for ts in range(NTS):
                    pb = t["pp"][ts % 4]
                    sm2 = p.inc("ymm")
                    ymm_snap.append(p.cnt["ymm"])
                    if p.on("tensor"):
                        if fc == 0 and ts == 0:
                            p.q.wait_ge(p.sems["vh"], vh_total)
                        p.q.wait_ge(p.sems["yld"], yld_snap[fc])
                        idx = fc * NTS + ts
                        if idx >= 4:
                            p.q.wait_ge(p.sems["yev"], idx - 3)
                        for et in range(NDT):
                            base = (fc % 2) * NDT * 512 + et * 512
                            ins = p.q.matmul(pb[:, 0:512],
                                             kv3[:, et, ts * 128:ts * 128 + 128],
                                             t["woc"][:, base:base + 512],
                                             start=(et == 0), stop=(et == NDT - 1))
                            if et == NDT - 1:
                                ins.then_inc(sm2, 1)
                    p.wait("scalar", "ymm")
                    if p.on("scalar"):
                        p.q.copy(t["ysb"][:, ts * 2048 + fc * 512:ts * 2048 + fc * 512 + 512],
                                 pb[:, 0:512])
                    p.fin("scalar", "yev")
            yev_total = p.cnt["yev"]

            xld2_snap = []
            res_snap = []
            sto_snap = []
            for ts in range(NTS):
                sx2 = p.inc("xld", 16)
                xld2_snap.append(p.cnt["xld"])
                if p.on("sync"):
                    p.q.wait_ge(p.sems["ln1"], ln1_snap[-1])
                    if ts >= 2:
                        p.q.wait_ge(p.sems["sto"], sto_snap[ts - 2])
                    p.q.dma_start(out=t["xbuf"][:, (ts % 2) * 2048:(ts % 2 + 1) * 2048],
                                  in_=x_in[ts * 128:(ts + 1) * 128, :]).then_inc(sx2, 16)

                def prew2(p2, ts=ts):
                    p2.wait("vector", "yev", yev_total)

                emit_ln_one(p, t,
                            t["ysb"][:, ts * 2048:(ts + 1) * 2048],
                            t["ysb"][:, ts * 2048:(ts + 1) * 2048],
                            8 + ts, "lny", pre_wait=prew2)
                if p.on("vector"):
                    p.q.wait_ge(p.sems["xld"], xld2_snap[ts])
                    p.q.tensor_tensor(out=t["xbuf"][:, (ts % 2) * 2048:(ts % 2 + 1) * 2048],
                                      in0=t["xbuf"][:, (ts % 2) * 2048:(ts % 2 + 1) * 2048],
                                      in1=t["ysb"][:, ts * 2048:(ts + 1) * 2048],
                                      op=ALU.add)
                p.fin("vector", "res")
                res_snap.append(p.cnt["res"])
                ss2 = p.inc("sto", 16)
                sto_snap.append(p.cnt["sto"])
                if p.on("sync"):
                    p.q.wait_ge(p.sems["res"], res_snap[ts])
                    p.q.dma_start(out=x_out[ts * 128:(ts + 1) * 128, :],
                                  in_=t["xbuf"][:, (ts % 2) * 2048:(ts % 2 + 1) * 2048]).then_inc(ss2, 16)
            p.wait("sync", "sto")

        for role in ROLES:
            getattr(block, role)(lambda q, role=role: program(P(role, q, sems)))

    return nc



def core_slices():
    return [(c // 4, c % 4, 7 - c % 4) for c in range(8)]


def pack_x(x):
    return [np.ascontiguousarray(
        np.concatenate([x[b, a * HC:(a + 1) * HC], x[b, bb * HC:(bb + 1) * HC]], axis=0))
        for b, a, bb in core_slices()]


def unpack_x(shards, dtype=np.float32):
    x = np.empty((B, S, D), dtype)
    for c, (b, a, bb) in enumerate(core_slices()):
        x[b, a * HC:(a + 1) * HC] = shards[c][:HC]
        x[b, bb * HC:(bb + 1) * HC] = shards[c][HC:]
    return x


def carry_masks():
    out = []
    for b, a, bb in core_slices():
        m = np.zeros((128, 16), np.float32)
        for side, h in ((0, a), (1, bb)):
            m[:, side * 8:side * 8 + h] = 1.0
        out.append(m)
    return out


_IDENT = np.eye(128, dtype=np.float32).astype(BF)


def run_l0_attn(x, Wq, Wk, Wv, Wo):
    wqT = np.ascontiguousarray(Wq.T)
    wkT = np.ascontiguousarray(Wk.T)
    wvT = np.ascontiguousarray(Wv.T)
    woT = np.ascontiguousarray(Wo.T)
    shards = pack_x(x)
    masks = carry_masks()
    nc = build_l0_attn()
    in_maps = [{"x_hc": shards[c], "wqT": wqT, "wkT": wkT, "wvT": wvT,
                "woT": woT, "ident_hbm": _IDENT, "cmask_hbm": masks[c]} for c in range(8)]
    res = run_bass_kernel_spmd(nc, in_maps, core_ids=list(range(8)))
    return unpack_x([res.results[c]["x1_hc"] for c in range(8)])





NFT = 8192 // 128
FCH = 4
NFC = NFT // FCH


def build_ffn(quant):
    nc = bass.Bass("TRN2", target_bir_lowering=False)
    WDT = I32 if quant else F32
    x_in = nc.dram_tensor("x_hc", [TOK, D], F32, kind="ExternalInput").ap()
    gw = nc.dram_tensor("gwT", [D, 8192], WDT, kind="ExternalInput").ap()
    uw = nc.dram_tensor("uwT", [D, 8192], WDT, kind="ExternalInput").ap()
    dw = nc.dram_tensor("dwT", [8192, D], WDT, kind="ExternalInput").ap()
    ident_in = nc.dram_tensor("ident_hbm", [128, 128], BF16, kind="ExternalInput").ap()
    if quant:
        gs_in = nc.dram_tensor("gs_hbm", [128, NFT], F32, kind="ExternalInput").ap()
        us_in = nc.dram_tensor("us_hbm", [128, NFT], F32, kind="ExternalInput").ap()
        ds_in = nc.dram_tensor("ds_hbm", [128, D], F32, kind="ExternalInput").ap()
    x_out = nc.dram_tensor("x2_hc", [TOK, D], F32, kind="ExternalOutput").ap()

    with ExitStack() as est:
        t = {}
        def sb(name, shape, dt):
            t[name] = est.enter_context(nc.sbuf_tensor(name, shape, dt))
        def ps(name, shape, dt):
            t[name] = est.enter_context(nc.psum_tensor(name, shape, dt))

        sb("xbuf", [128, 2 * 2048], F32)
        sb("lnx", [128, 2 * 2048], BF16)
        sb("lnxT", [128, NDT * TOK], BF16)
        sb("zsb", [128, NTS * 2048], F32)
        sb("hball", [128, FCH * TOK], BF16)
        sb("hg", [128, 2 * TOK], BF16)
        sb("wraw", [128, 2 * NDT * 128], WDT)
        sb("wbf", [128, 4 * NDT * 128], BF16)
        sb("dwraw", [128, 2 * 2048], WDT)
        sb("dwbf", [128, FCH * 2048], BF16)
        sb("ident", [128, 128], BF16)
        if quant:
            sb("gsb", [128, NFT], F32)
            sb("usb", [128, NFT], F32)
            sb("dsb", [128, D], F32)
        sb("stats", [128, 24], F32)
        sb("aggr", [128, 32], F32)
        sb("veps", [128, 16], F32)
        sb("recv", [128, 16], F32)
        sb("negm", [128, 16], F32)
        sb("rstd", [128, 16], F32)

        for i in range(4):
            ps(f"pp{i}", [128, 512], F32)
        t["pp"] = [t[f"pp{i}"] for i in range(4)]
        ps("pz0", [128, 512], F32)
        ps("pz1", [128, 512], F32)
        t["pz"] = [t["pz0"], t["pz1"]]
        ps("ptp0", [128, 1024], BF16)
        ps("ptp1", [128, 1024], BF16)
        t["ptp"] = [t["ptp0"], t["ptp1"]]

        sem_names = ["xld", "wld", "wc", "dwld", "dwc", "lnv", "lns", "ln1",
                     "tp", "tpe", "mm", "hev", "dmm", "zad", "lny", "res", "sto"]
        sems = {n: est.enter_context(nc.semaphore(n)) for n in sem_names}
        block = est.enter_context(nc.Block())

        def program(p):
            lnxT = t["lnxT"]
            lnxT3 = lnxT.rearrange("p (dt tok) -> p dt tok", dt=NDT)
            wraw2 = t["wraw"].rearrange("p (s dt c) -> p s dt c", s=2, dt=NDT)
            wbf4 = t["wbf"].rearrange("p (s dt c) -> p s dt c", s=4, dt=NDT)

            n_pre = 1 + (3 if quant else 0)
            s0 = p.inc("xld", 16 * n_pre)
            if p.on("sync"):
                p.q.dma_start(out=t["ident"][:, :], in_=ident_in[:, :]).then_inc(s0, 16)
                if quant:
                    p.q.dma_start(out=t["gsb"][:, :], in_=gs_in[:, :]).then_inc(s0, 16)
                    p.q.dma_start(out=t["usb"][:, :], in_=us_in[:, :]).then_inc(s0, 16)
                    p.q.dma_start(out=t["dsb"][:, :], in_=ds_in[:, :]).then_inc(s0, 16)
            prelude_xld = p.cnt["xld"]
            p.wait("tensor", "xld", prelude_xld)

            xld_snap, ln1_snap, tp_snap = [], [], []
            for ts in range(NTS):
                sx = p.inc("xld", 16)
                xld_snap.append(p.cnt["xld"])
                if p.on("sync"):
                    if ts >= 2:
                        p.q.wait_ge(p.sems["ln1"], ln1_snap[ts - 2])
                    p.q.dma_start(out=t["xbuf"][:, (ts % 2) * 2048:(ts % 2 + 1) * 2048],
                                  in_=x_in[ts * 128:(ts + 1) * 128, :]).then_inc(sx, 16)

                def prew(p2, ts=ts):
                    p2.wait("vector", "xld", xld_snap[ts])
                    if ts >= 2:
                        p2.wait("vector", "tp", tp_snap[ts - 2][0])

                emit_ln_one(p, t,
                            t["xbuf"][:, (ts % 2) * 2048:(ts % 2 + 1) * 2048],
                            t["lnx"][:, (ts % 2) * 2048:(ts % 2 + 1) * 2048],
                            ts, "ln1", pre_wait=prew)
                ln1_snap.append(p.cnt["ln1"])
                p.wait("tensor", "ln1", ln1_snap[ts])
                emit_transpose_2048(p, t, ts,
                                    t["lnx"][:, (ts % 2) * 2048:(ts % 2 + 1) * 2048],
                                    lnxT, tp_snap)
            tpe_total = p.cnt["tpe"]

            RING = 4
            wld_snap, wc_snap = [], []
            dwld_snap = []
            for fc in range(NFC):
                for k in range(FCH):
                    ft = fc * FCH + k
                    for wi in range(2):
                        ci = 2 * ft + wi
                        sw = p.inc("wld", 16)
                        wld_snap.append(p.cnt["wld"])
                        if p.on("gpsimd"):
                            if ci >= 2:
                                p.q.wait_ge(p.sems["wc"], ci - 1)
                            srcw = (gw if wi == 0 else uw).rearrange(
                                "(dt pp) f -> pp dt f", pp=128)[:, :, ft * 128:(ft + 1) * 128]
                            p.q.dma_start(out=wraw2[:, ci % 2], in_=srcw).then_inc(sw, 16)
                for k in range(FCH):
                    di = fc * FCH + k
                    sdw = p.inc("dwld", 16)
                    dwld_snap.append(p.cnt["dwld"])
                    if p.on("gpsimd"):
                        if di >= 2:
                            p.q.wait_ge(p.sems["dwc"], di - 1)
                        p.q.dma_start(out=t["dwraw"][:, (di % 2) * 2048:(di % 2 + 1) * 2048],
                                      in_=dw[di * 128:(di + 1) * 128, :]).then_inc(sdw, 16)

            sevac_snap = []
            hmult_snap = []
            dmm_snap = [0]
            zad = [0]
            for fc in range(NFC):
                for k in range(FCH):
                    ft = fc * FCH + k
                    for wi in range(2):
                        ci = 2 * ft + wi
                        p.wait("vector", "wld", wld_snap[ci])
                        if p.on("vector"):
                            if ci >= RING:
                                p.q.wait_ge(p.sems["mm"], 2 * (ci - RING + 1))
                            p.q.tensor_scalar(
                                out=wbf4[:, ci % RING].rearrange("p dt c -> p (dt c)"),
                                in0=wraw2[:, ci % 2].rearrange("p dt c -> p (dt c)"),
                                scalar1=(-128.0 if quant else 0.0), scalar2=None, op0=ALU.add)
                        p.fin("vector", "wc")
                        wc_snap.append(p.cnt["wc"])
                        for th in range(2):
                            pr = 2 * ci + th
                            pb = t["pp"][pr % 4]
                            sm = p.inc("mm")
                            if p.on("tensor"):
                                p.q.wait_ge(p.sems["wc"], wc_snap[ci])
                                if pr == 0:
                                    p.q.wait_ge(p.sems["tpe"], tpe_total)
                                if pr >= 4:
                                    p.q.wait_ge(p.sems["hev"], sevac_snap[pr - 4])
                                for dt in range(NDT):
                                    ins = p.q.matmul(pb[:, 0:512], wbf4[:, ci % RING, dt],
                                                     lnxT3[:, dt, th * 512:(th + 1) * 512],
                                                     start=(dt == 0), stop=(dt == NDT - 1))
                                    if dt == NDT - 1:
                                        ins.then_inc(sm, 1)
                            p.wait("scalar", "mm")
                            if p.on("scalar"):
                                if wi == 0 and th == 0 and ft >= 1:
                                    p.q.wait_ge(p.sems["hev"], hmult_snap[ft - 1])
                                col = (wi * 1024) + th * 512
                                if quant:
                                    p.q.activation(out=t["hg"][:, col:col + 512],
                                                   in_=pb[:, 0:512],
                                                   func=(AF.Silu if wi == 0 else AF.Copy),
                                                   scale=(t["gsb"] if wi == 0 else t["usb"])[:, ft:ft + 1])
                                elif wi == 0:
                                    p.q.activation(out=t["hg"][:, col:col + 512],
                                                   in_=pb[:, 0:512], func=AF.Silu)
                                else:
                                    p.q.copy(t["hg"][:, col:col + 512], pb[:, 0:512])
                            p.fin("scalar", "hev")
                            sevac_snap.append(p.cnt["hev"])
                    p.wait("vector", "hev")
                    if p.on("vector"):
                        if fc >= 1:
                            p.q.wait_ge(p.sems["dmm"], dmm_snap[-1])
                        p.q.tensor_tensor(out=t["hball"][:, k * 1024:(k + 1) * 1024],
                                          in0=t["hg"][:, 0:1024], in1=t["hg"][:, 1024:2048],
                                          op=ALU.mult)
                    p.fin("vector", "hev")
                    hmult_snap.append(p.cnt["hev"])

                for k in range(FCH):
                    di = fc * FCH + k
                    p.wait("vector", "dwld", dwld_snap[di])
                    if p.on("vector"):
                        if fc >= 1 and k == 0:
                            p.q.wait_ge(p.sems["dmm"], dmm_snap[-1])
                        p.q.tensor_scalar(out=t["dwbf"][:, k * 2048:(k + 1) * 2048],
                                          in0=t["dwraw"][:, (di % 2) * 2048:(di % 2 + 1) * 2048],
                                          scalar1=(-128.0 if quant else 0.0), scalar2=None,
                                          op0=ALU.add)
                    p.fin("vector", "dwc")
                dwc_needed = p.cnt["dwc"]
                hev_needed = hmult_snap[-1]

                for tsub in range(NTS):
                    for outc in range(4):
                        gidx = fc * 32 + tsub * 4 + outc
                        pz = t["pz"][gidx % 2]
                        sd = p.inc("dmm")
                        if p.on("tensor"):
                            p.q.wait_ge(p.sems["hev"], hev_needed)
                            p.q.wait_ge(p.sems["dwc"], dwc_needed)
                            if gidx >= 2:
                                p.q.wait_ge(p.sems["zad"], gidx - 1)
                            for k in range(FCH):
                                ins = p.q.matmul(
                                    pz[:, 0:512],
                                    t["hball"][:, k * 1024 + tsub * 128:k * 1024 + tsub * 128 + 128],
                                    t["dwbf"][:, k * 2048 + outc * 512:k * 2048 + outc * 512 + 512],
                                    start=(k == 0), stop=(k == FCH - 1))
                                if k == FCH - 1:
                                    ins.then_inc(sd, 1)
                        p.wait("vector", "dmm")
                        if p.on("vector"):
                            zslice = t["zsb"][:, tsub * 2048 + outc * 512:tsub * 2048 + outc * 512 + 512]
                            if fc == 0:
                                p.q.tensor_copy(zslice, pz[:, 0:512])
                            else:
                                p.q.tensor_tensor(out=zslice, in0=zslice, in1=pz[:, 0:512],
                                                  op=ALU.add)
                        p.fin("vector", "zad")
                dmm_snap.append(p.cnt["dmm"])
            zad_total = p.cnt["zad"]

            xld2_snap, res_snap, sto_snap = [], [], []
            for ts in range(NTS):
                sx2 = p.inc("xld", 16)
                xld2_snap.append(p.cnt["xld"])
                if p.on("sync"):
                    p.q.wait_ge(p.sems["ln1"], ln1_snap[-1])
                    if ts >= 2:
                        p.q.wait_ge(p.sems["sto"], sto_snap[ts - 2])
                    p.q.dma_start(out=t["xbuf"][:, (ts % 2) * 2048:(ts % 2 + 1) * 2048],
                                  in_=x_in[ts * 128:(ts + 1) * 128, :]).then_inc(sx2, 16)

                if quant:
                    p.wait("vector", "zad", zad_total)
                    if p.on("vector"):
                        p.q.tensor_tensor(out=t["zsb"][:, ts * 2048:(ts + 1) * 2048],
                                          in0=t["zsb"][:, ts * 2048:(ts + 1) * 2048],
                                          in1=t["dsb"][:, :], op=ALU.mult)
                    p.fin("vector", "zad")

                def prew2(p2, ts=ts):
                    p2.wait("vector", "zad", zad_total)

                emit_ln_one(p, t,
                            t["zsb"][:, ts * 2048:(ts + 1) * 2048],
                            t["zsb"][:, ts * 2048:(ts + 1) * 2048],
                            8 + ts, "lny", pre_wait=prew2)
                if p.on("vector"):
                    p.q.wait_ge(p.sems["xld"], xld2_snap[ts])
                    p.q.tensor_tensor(out=t["xbuf"][:, (ts % 2) * 2048:(ts % 2 + 1) * 2048],
                                      in0=t["xbuf"][:, (ts % 2) * 2048:(ts % 2 + 1) * 2048],
                                      in1=t["zsb"][:, ts * 2048:(ts + 1) * 2048],
                                      op=ALU.add)
                p.fin("vector", "res")
                res_snap.append(p.cnt["res"])
                ss2 = p.inc("sto", 16)
                sto_snap.append(p.cnt["sto"])
                if p.on("sync"):
                    p.q.wait_ge(p.sems["res"], res_snap[ts])
                    p.q.dma_start(out=x_out[ts * 128:(ts + 1) * 128, :],
                                  in_=t["xbuf"][:, (ts % 2) * 2048:(ts % 2 + 1) * 2048]).then_inc(ss2, 16)
            p.wait("sync", "sto")

        for role in ROLES:
            getattr(block, role)(lambda q, role=role: program(P(role, q, sems)))

    return nc


def _scales_tiles(s):
    return np.ascontiguousarray(s.reshape(-1, 128).T)


def run_ffn(x, gwT, uwT, dwT, quant, gs=None, us=None, ds=None):
    shards = pack_x(x)
    nc = build_ffn(quant)
    base = {"gwT": gwT, "uwT": uwT, "dwT": dwT, "ident_hbm": _IDENT}
    if quant:
        base["gs_hbm"] = _scales_tiles(gs)
        base["us_hbm"] = _scales_tiles(us)
        base["ds_hbm"] = np.ascontiguousarray(np.broadcast_to(ds[None, :], (128, D)))
    in_maps = [dict(base, x_hc=shards[c]) for c in range(8)]
    res = run_bass_kernel_spmd(nc, in_maps, core_ids=list(range(8)))
    return unpack_x([res.results[c]["x2_hc"] for c in range(8)])




NH = 16
DHD = 128
DKV = 512
NCT = DKV // 128
RSCL = float(1.0 / np.sqrt(np.float32(DHD)))


def build_l1_attn():
    nc = bass.Bass("TRN2", target_bir_lowering=False)
    x_in = nc.dram_tensor("x_hc", [TOK, D], F32, kind="ExternalInput").ap()
    wdq = nc.dram_tensor("wdq", [D, DKV], F32, kind="ExternalInput").ap()
    wdkv = nc.dram_tensor("wdkv", [D, DKV], F32, kind="ExternalInput").ap()
    wuq = nc.dram_tensor("wuq", [DKV, 1024], F32, kind="ExternalInput").ap()
    wqr = nc.dram_tensor("wqr", [DKV, 1024], F32, kind="ExternalInput").ap()
    wuk = nc.dram_tensor("wuk", [DKV, 1024], F32, kind="ExternalInput").ap()
    wuv = nc.dram_tensor("wuv", [DKV, 2048], F32, kind="ExternalInput").ap()
    wkr = nc.dram_tensor("wkr", [D, 1024], F32, kind="ExternalInput").ap()
    wout = nc.dram_tensor("wout", [D, D], F32, kind="ExternalInput").ap()
    gq_in = nc.dram_tensor("gq_hbm", [128, NCT], F32, kind="ExternalInput").ap()
    gkv_in = nc.dram_tensor("gkv_hbm", [128, NCT], F32, kind="ExternalInput").ap()
    bout_in = nc.dram_tensor("bout_hbm", [128, D], F32, kind="ExternalInput").ap()
    cs_in = nc.dram_tensor("cs_hbm", [128, TOK], F32, kind="ExternalInput").ap()
    tri_in = nc.dram_tensor("tri_hbm", [128, 4 * 512], BF16, kind="ExternalInput").ap()
    idx_in = nc.dram_tensor("idx_hbm", [128, 192], mybir.dt.uint32, kind="ExternalInput").ap()
    onesc_in = nc.dram_tensor("onesc_hbm", [128, 1], BF16, kind="ExternalInput").ap()
    onesr_in = nc.dram_tensor("onesr_hbm", [1, 128], BF16, kind="ExternalInput").ap()
    ident_in = nc.dram_tensor("ident_hbm", [128, 128], BF16, kind="ExternalInput").ap()
    x_out = nc.dram_tensor("x3_hc", [TOK, D], F32, kind="ExternalOutput").ap()

    kT_in = nc.dram_tensor("kT_in", [2048, 1024], BF16)
    v_in = nc.dram_tensor("v_in", [1024, 2048], BF16)
    q_in = nc.dram_tensor("q_in", [2048, 1024], BF16)
    kT_gat = nc.dram_tensor("kT_gat", [8192, 1024], BF16)
    v_gat = nc.dram_tensor("v_gat", [4096, 2048], BF16)
    q_gat = nc.dram_tensor("q_gat", [8192, 1024], BF16)
    o_in = nc.dram_tensor("o_in", [512, 4096], BF16)
    o_gat = nc.dram_tensor("o_gat", [2048, 4096], BF16)

    with ExitStack() as est:
        t = {}
        def sb(name, shape, dt):
            t[name] = est.enter_context(nc.sbuf_tensor(name, shape, dt))
        def ps(name, shape, dt):
            t[name] = est.enter_context(nc.psum_tensor(name, shape, dt))

        sb("xbuf", [128, 2 * 2048], F32)
        sb("lnx", [128, 2 * 2048], BF16)
        sb("lnxT", [128, NDT * TOK], BF16)
        sb("arena", [128, 32768], BF16)
        sb("cqT", [128, NCT * TOK], BF16)
        sb("ckvT", [128, NCT * TOK], BF16)
        sb("sq", [128, 512], BF16)
        sb("rowf", [1, 2 * 512], F32)
        sb("rowb", [1, 2 * 512], BF16)
        sb("rstdb", [128, 2 * 512], BF16)
        sb("stg", [128, 4 * 512], BF16)
        sb("kcache", [128, 8 * 512], BF16)
        sb("vcache", [128, 8 * 512], BF16)
        sb("qtile", [128, 2 * 512], BF16)
        sb("ptile", [128, 2 * 512], BF16)
        sb("obuf", [128, 2 * 512], BF16)
        sb("aoT", [128, NDT * 512], BF16)
        sb("csT", [128, TOK], F32)
        sb("tri", [128, 4 * 512], BF16)
        sb("idx", [128, 192], mybir.dt.uint32)
        sb("gq", [128, NCT], F32)
        sb("gkv", [128, NCT], F32)
        sb("onesc", [128, 1], BF16)
        sb("onesr", [1, 128], BF16)
        sb("ident", [128, 128], BF16)
        sb("stats", [128, 24], F32)
        sb("aggr", [128, 32], F32)
        sb("veps", [128, 16], F32)
        sb("recv", [128, 16], F32)
        sb("negm", [128, 16], F32)
        sb("rstd", [128, 16], F32)

        for i in range(4):
            ps(f"pp{i}", [128, 512], F32)
        t["pp"] = [t[f"pp{i}"] for i in range(4)]
        ps("prow", [1, 512], F32)
        ps("ptp0", [128, 1024], BF16)
        ps("ptp1", [128, 1024], BF16)
        t["ptp"] = [t["ptp0"], t["ptp1"]]

        sem_names = ["xld", "aux", "wst", "lnv", "lns", "ln1", "tp", "tpe",
                     "mm", "ev", "stg", "sdma", "cc", "fet",
                     "att", "oev", "odma", "wold", "ymm", "yev",
                     "lny", "res", "sto"]
        sems = {n: est.enter_context(nc.semaphore(n)) for n in sem_names}
        block = est.enter_context(nc.Block())

        A_WDQ = 0
        A_WDKV = 8192
        A_WKR = 0
        A_WUV = 16384
        A_WUK = 24576
        A_WUQ = 24576
        A_WQR = 28672
        A_WOC = 0
        A_BOUT = 16384

        def program(p):
            lnxT = t["lnxT"]
            lnxT3 = lnxT.rearrange("p (dt tok) -> p dt tok", dt=NDT)
            ar = t["arena"]

            s0 = p.inc("aux", 16 * 9)
            if p.on("sync"):
                p.q.dma_start(out=t["ident"][:, :], in_=ident_in[:, :]).then_inc(s0, 16)
                p.q.dma_start(out=t["csT"][:, :], in_=cs_in[:, :]).then_inc(s0, 16)
                p.q.dma_start(out=t["tri"][:, :], in_=tri_in[:, :]).then_inc(s0, 16)
                p.q.dma_start(out=t["idx"][:, :], in_=idx_in[:, :]).then_inc(s0, 16)
                p.q.dma_start(out=t["gq"][:, :], in_=gq_in[:, :]).then_inc(s0, 16)
                p.q.dma_start(out=t["gkv"][:, :], in_=gkv_in[:, :]).then_inc(s0, 16)
                p.q.dma_start(out=t["onesc"][:, :], in_=onesc_in[:, :]).then_inc(s0, 16)
                p.q.dma_start(out=t["onesr"][:, :], in_=onesr_in[:, :]).then_inc(s0, 16)
                p.q.dma_start(out=t["ident"][:, :], in_=ident_in[:, :]).then_inc(s0, 16)
            aux_pre = p.cnt["aux"]
            p.wait("tensor", "aux", aux_pre)

            sw1 = p.inc("wst", 32)
            if p.on("gpsimd"):
                p.q.dma_start(out=ar.rearrange("p (a b) -> p a b", a=64)[:, 0:16].rearrange("p a b -> p (a b)").rearrange("p (dt c) -> p dt c", dt=NDT),
                              in_=wdq.rearrange("(dt pp) c -> pp dt c", pp=128)).then_inc(sw1, 16)
                p.q.dma_start(out=ar[:, A_WDKV:A_WDKV + 8192].rearrange("p (dt c) -> p dt c", dt=NDT),
                              in_=wdkv.rearrange("(dt pp) c -> pp dt c", pp=128)).then_inc(sw1, 16)
            wst1 = p.cnt["wst"]

            xld_snap, ln1_snap, tp_snap = [], [], []
            for ts in range(NTS):
                sx = p.inc("xld", 16)
                xld_snap.append(p.cnt["xld"])
                if p.on("sync"):
                    if ts >= 2:
                        p.q.wait_ge(p.sems["ln1"], ln1_snap[ts - 2])
                    p.q.dma_start(out=t["xbuf"][:, (ts % 2) * 2048:(ts % 2 + 1) * 2048],
                                  in_=x_in[ts * 128:(ts + 1) * 128, :]).then_inc(sx, 16)

                def prew(p2, ts=ts):
                    p2.wait("vector", "xld", xld_snap[ts])
                    if ts >= 2:
                        p2.wait("vector", "tp", tp_snap[ts - 2][0])

                emit_ln_one(p, t,
                            t["xbuf"][:, (ts % 2) * 2048:(ts % 2 + 1) * 2048],
                            t["lnx"][:, (ts % 2) * 2048:(ts % 2 + 1) * 2048],
                            ts, "ln1", pre_wait=prew)
                ln1_snap.append(p.cnt["ln1"])
                p.wait("tensor", "ln1", ln1_snap[ts])
                emit_transpose_2048(p, t, ts,
                                    t["lnx"][:, (ts % 2) * 2048:(ts % 2 + 1) * 2048],
                                    lnxT, tp_snap)
            tpe_total = p.cnt["tpe"]

            def emit_c(which):
                woff = A_WDQ if which == "q" else A_WDKV
                raw = t["cqT"] if which == "q" else t["ckvT"]
                c_evac_snap = []
                sq_mm_snap = []
                outT = t["cqT"] if which == "q" else t["ckvT"]
                gsc = t["gq"] if which == "q" else t["gkv"]
                ar3 = ar[:, woff:woff + 8192].rearrange("p (dt c) -> p dt c", dt=NDT)
                for th in range(2):
                    for ct in range(NCT):
                        pr = th * NCT + ct
                        pb = t["pp"][pr % 2]
                        sm = p.inc("mm")
                        if p.on("tensor"):
                            p.q.wait_ge(p.sems["wst"], wst1)
                            if pr == 0 and which == "q":
                                p.q.wait_ge(p.sems["tpe"], tpe_total)
                            if len(c_evac_snap) >= 2:
                                p.q.wait_ge(p.sems["ev"], c_evac_snap[-2])
                            for dt in range(NDT):
                                ins = p.q.matmul(pb[:, 0:512], ar3[:, dt, ct * 128:(ct + 1) * 128],
                                                 lnxT3[:, dt, th * 512:(th + 1) * 512],
                                                 start=(dt == 0), stop=(dt == NDT - 1))
                                if dt == NDT - 1:
                                    ins.then_inc(sm, 1)
                        p.wait("vector", "mm")
                        if p.on("vector"):
                            p.q.tensor_copy(raw[:, ct * TOK + th * 512:ct * TOK + th * 512 + 512],
                                            pb[:, 0:512])
                        p.fin("vector", "ev")
                        c_evac_snap.append(p.cnt["ev"])
                        p.wait("scalar", "ev")
                        if p.on("scalar"):
                            if sq_mm_snap:
                                p.q.wait_ge(p.sems["mm"], sq_mm_snap[-1])
                        if p.on("scalar"):
                            p.q.activation(out=t["sq"][:, 0:512],
                                           in_=raw[:, ct * TOK + th * 512:ct * TOK + th * 512 + 512],
                                           func=AF.Square)
                        p.fin("scalar", "ev")
                        sm2 = p.inc("mm")
                        sq_mm_snap.append(p.cnt["mm"])
                        if p.on("tensor"):
                            p.q.wait_ge(p.sems["ev"], p.cnt["ev"])
                            p.q.matmul(t["prow"][:, 0:512], t["onesc"][:, :], t["sq"][:, 0:512],
                                       start=(ct == 0), stop=(ct == NCT - 1)).then_inc(sm2, 1)
                    p.wait("vector", "mm")
                    if p.on("vector"):
                        p.q.tensor_scalar(out=t["rowf"][:, th * 512:th * 512 + 512],
                                          in0=t["prow"][:, 0:512],
                                          scalar1=1.0 / 512.0, scalar2=EPS,
                                          op0=ALU.mult, op1=ALU.add)
                        p.q.drain()
                        p.q.reciprocal(t["rowf"][:, th * 512:th * 512 + 512],
                                       t["rowf"][:, th * 512:th * 512 + 512])
                    p.fin("vector", "ev")
                    p.wait("scalar", "ev")
                    if p.on("scalar"):
                        p.q.activation(out=t["rowb"][:, th * 512:th * 512 + 512],
                                       in_=t["rowf"][:, th * 512:th * 512 + 512], func=AF.Sqrt)
                    p.fin("scalar", "ev")
                    sm3 = p.inc("mm")
                    if p.on("tensor"):
                        p.q.wait_ge(p.sems["ev"], p.cnt["ev"])
                        p.q.matmul(t["pp"][2][:, 0:512], t["onesr"][:, :],
                                   t["rowb"][:, th * 512:th * 512 + 512],
                                   start=True, stop=True).then_inc(sm3, 1)
                    p.wait("vector", "mm")
                    if p.on("vector"):
                        p.q.tensor_copy(t["rstdb"][:, th * 512:th * 512 + 512], t["pp"][2][:, 0:512])
                    p.fin("vector", "ev")
                    for ct in range(NCT):
                        if p.on("vector"):
                            p.q.scalar_tensor_tensor(
                                out=outT[:, ct * TOK + th * 512:ct * TOK + th * 512 + 512],
                                in0=raw[:, ct * TOK + th * 512:ct * TOK + th * 512 + 512],
                                scalar=gsc[:, ct:ct + 1],
                                in1=t["rstdb"][:, th * 512:th * 512 + 512],
                                op0=ALU.mult, op1=ALU.mult)
                            p.q.drain()
                    p.fin("vector", "ev")

            emit_c("q")
            emit_c("kv")
            cq_done_ev = p.cnt["ev"]
            cq_done_mm = p.cnt["mm"]

            sw2 = p.inc("wst", 48)
            if p.on("gpsimd"):
                p.q.wait_ge(p.sems["mm"], cq_done_mm)
                p.q.dma_start(out=ar[:, A_WKR:A_WKR + 16384].rearrange("p (dt c) -> p dt c", dt=NDT),
                              in_=wkr.rearrange("(dt pp) c -> pp dt c", pp=128)).then_inc(sw2, 16)
                p.q.dma_start(out=ar[:, A_WUV:A_WUV + 8192].rearrange("p (ct c) -> p ct c", ct=NCT),
                              in_=wuv.rearrange("(ct pp) c -> pp ct c", pp=128)).then_inc(sw2, 16)
                p.q.dma_start(out=ar[:, A_WUK:A_WUK + 4096].rearrange("p (ct c) -> p ct c", ct=NCT),
                              in_=wuk.rearrange("(ct pp) c -> pp ct c", pp=128)).then_inc(sw2, 16)
            wst2 = p.cnt["wst"]
            wkr3 = ar[:, A_WKR:A_WKR + 16384].rearrange("p (dt c) -> p dt c", dt=NDT)
            wuv3 = ar[:, A_WUV:A_WUV + 8192].rearrange("p (ct c) -> p ct c", ct=NCT)
            wuk3 = ar[:, A_WUK:A_WUK + 4096].rearrange("p (ct c) -> p ct c", ct=NCT)

            stg_ring = 4
            stg_idx = [0]

            def stage_out(dst_ap, evac_lo, evac_hi=None):
                si = stg_idx[0]
                stg_idx[0] += 1
                slot = t["stg"][:, (si % stg_ring) * 512:(si % stg_ring) * 512 + 512]
                if si >= stg_ring:
                    p.wait("scalar", "sdma", 16 * (si - stg_ring + 1))
                    p.wait("vector", "sdma", 16 * (si - stg_ring + 1))
                evac_lo(slot)
                p.fin("scalar", "stg")
                if evac_hi is not None:
                    evac_hi(slot)
                    p.fin("vector", "stg")
                p.wait("sync", "stg")
                sd = p.inc("sdma", 16)
                if p.on("sync"):
                    p.q.dma_start(out=dst_ap, in_=slot).then_inc(sd, 16)

            kproj_stg = []
            for ft in range(NH):
                for th in range(2):
                    pr = ft * 2 + th
                    pb = t["pp"][pr % 2]
                    sm = p.inc("mm")
                    if p.on("tensor"):
                        p.q.wait_ge(p.sems["wst"], wst2)
                        if len(kproj_stg) >= 2:
                            p.q.wait_ge(p.sems["stg"], kproj_stg[-2])
                        for ct in range(NCT):
                            p.q.matmul(pb[0:64, 0:512], wuk3[:, ct, ft * 64:(ft + 1) * 64],
                                       ckvT3_ := t["ckvT"].rearrange("p (ct tok) -> p ct tok", ct=NCT)[:, ct, th * 512:(th + 1) * 512],
                                       start=(ct == 0), stop=(ct == NCT - 1))
                        for dt in range(NDT):
                            ins = p.q.matmul(pb[64:128, 0:512], wkr3[:, dt, ft * 64:(ft + 1) * 64],
                                             lnxT3[:, dt, th * 512:(th + 1) * 512],
                                             start=(dt == 0), stop=(dt == NDT - 1))
                            if dt == NDT - 1:
                                ins.then_inc(sm, 1)

                    def lo(slot, pb=pb):
                        p.wait("scalar", "mm")
                        if p.on("scalar"):
                            p.q.copy(slot[0:64, :], pb[0:64, 0:512])

                    def hi(slot, pb=pb, th=th):
                        p.wait("vector", "mm")
                        if p.on("vector"):
                            p.q.tensor_tensor(out=slot[64:128, :], in0=pb[64:128, 0:512],
                                              in1=t["csT"][64:128, th * 512:(th + 1) * 512],
                                              op=ALU.mult)

                    stage_out(kT_in[ft * 128:(ft + 1) * 128, th * 512:(th + 1) * 512], lo, hi)
                    kproj_stg.append(p.cnt["stg"])

            ckvT3 = t["ckvT"].rearrange("p (ct tok) -> p ct tok", ct=NCT)
            vproj_stg = []
            for tsub in range(NTS):
                for fc in range(4):
                    pr = tsub * 4 + fc
                    pb = t["pp"][2 + pr % 2]
                    sm = p.inc("mm")
                    if p.on("tensor"):
                        if len(vproj_stg) >= 2:
                            p.q.wait_ge(p.sems["stg"], vproj_stg[-2])
                        for ct in range(NCT):
                            ins = p.q.matmul(pb[:, 0:512],
                                             ckvT3[:, ct, tsub * 128:tsub * 128 + 128],
                                             wuv3[:, ct, fc * 512:(fc + 1) * 512],
                                             start=(ct == 0), stop=(ct == NCT - 1))
                            if ct == NCT - 1:
                                ins.then_inc(sm, 1)

                    def lo(slot, pb=pb):
                        p.wait("scalar", "mm")
                        if p.on("scalar"):
                            p.q.copy(slot[:, :], pb[:, 0:512])

                    stage_out(v_in[tsub * 128:(tsub + 1) * 128, fc * 512:(fc + 1) * 512], lo)
                    vproj_stg.append(p.cnt["stg"])

            kv_done_mm = p.cnt["mm"]
            sw3 = p.inc("wst", 32)
            if p.on("gpsimd"):
                p.q.wait_ge(p.sems["mm"], kv_done_mm)
                p.q.dma_start(out=ar[:, A_WUQ:A_WUQ + 4096].rearrange("p (ct c) -> p ct c", ct=NCT),
                              in_=wuq.rearrange("(ct pp) c -> pp ct c", pp=128)).then_inc(sw3, 16)
                p.q.dma_start(out=ar[:, A_WQR:A_WQR + 4096].rearrange("p (ct c) -> p ct c", ct=NCT),
                              in_=wqr.rearrange("(ct pp) c -> pp ct c", pp=128)).then_inc(sw3, 16)
            wst3 = p.cnt["wst"]
            wuq3 = ar[:, A_WUQ:A_WUQ + 4096].rearrange("p (ct c) -> p ct c", ct=NCT)
            wqr3 = ar[:, A_WQR:A_WQR + 4096].rearrange("p (ct c) -> p ct c", ct=NCT)
            cqT3 = t["cqT"].rearrange("p (ct tok) -> p ct tok", ct=NCT)

            qproj_stg = []
            for ft in range(NH):
                for th in range(2):
                    pr = ft * 2 + th
                    pb = t["pp"][pr % 2]
                    sm = p.inc("mm")
                    if p.on("tensor"):
                        p.q.wait_ge(p.sems["wst"], wst3)
                        if len(qproj_stg) >= 2:
                            p.q.wait_ge(p.sems["stg"], qproj_stg[-2])
                        for ct in range(NCT):
                            p.q.matmul(pb[0:64, 0:512], wuq3[:, ct, ft * 64:(ft + 1) * 64],
                                       cqT3[:, ct, th * 512:(th + 1) * 512],
                                       start=(ct == 0), stop=(ct == NCT - 1))
                        for ct in range(NCT):
                            ins = p.q.matmul(pb[64:128, 0:512], wqr3[:, ct, ft * 64:(ft + 1) * 64],
                                             cqT3[:, ct, th * 512:(th + 1) * 512],
                                             start=(ct == 0), stop=(ct == NCT - 1))
                            if ct == NCT - 1:
                                ins.then_inc(sm, 1)

                    def lo(slot, pb=pb):
                        p.wait("scalar", "mm")
                        if p.on("scalar"):
                            p.q.activation(out=slot[0:64, :], in_=pb[0:64, 0:512],
                                           func=AF.Copy, scale=RSCL)

                    def hi(slot, pb=pb, th=th):
                        p.wait("vector", "mm")
                        if p.on("vector"):
                            p.q.scalar_tensor_tensor(
                                out=slot[64:128, :], in0=pb[64:128, 0:512], scalar=RSCL,
                                in1=t["csT"][64:128, th * 512:(th + 1) * 512],
                                op0=ALU.mult, op1=ALU.mult)

                    stage_out(q_in[ft * 128:(ft + 1) * 128, th * 512:(th + 1) * 512], lo, hi)
                    qproj_stg.append(p.cnt["stg"])

            sdma_total = p.cnt["sdma"]
            for (src_t, dst_t) in ((kT_in, kT_gat), (v_in, v_gat), (q_in, q_gat)):
                scc = p.inc("cc")
                if p.on("gpsimd"):
                    if p.cnt["cc"] == 1:
                        p.q.wait_ge(p.sems["sdma"], sdma_total)
                    p.q.collective_compute(
                        "AllGather", ALU.bypass, replica_groups=GROUPS,
                        ins=[src_t.ap().opt()], outs=[dst_t.ap().opt()]).then_inc(scc, 1)
            cc3 = p.cnt["cc"]

            idx3 = t["idx"]
            IDX_K = 0
            IDX_V = 32
            IDX_O = 160
            v_flat = v_gat.rearrange("r (hb c) -> (r hb) c", hb=16)
            k_flat = kT_gat.rearrange("r (s c) -> (r s) c", s=2)
            q_flat = q_gat.rearrange("r (s c) -> (r s) c", s=2)
            vcache4 = t["vcache"].rearrange("p (khc sub c) -> p khc sub c", khc=8, sub=4)
            head_mm_snap = []
            qtile_mm_snap = {}
            exp_att_snap = []
            pexp_mm_snap = []
            for h in range(4):
                fk = p.inc("fet", 16 * 8)
                if p.on("gpsimd"):
                    if h == 0:
                        p.q.wait_ge(p.sems["cc"], cc3)
                    if h >= 1:
                        p.q.wait_ge(p.sems["mm"], head_mm_snap[h - 1])
                    for khc in range(8):
                        side = 0 if khc <= 3 else 1
                        p.q.indirect_dma_start(
                            out=t["kcache"][:, khc * 512:(khc + 1) * 512], out_offset=None,
                            in_=k_flat,
                            in_offset=bass.IndirectOffsetOnAxis(
                                ap=idx3[:, IDX_K + h * 8 + khc:IDX_K + h * 8 + khc + 1], axis=0),
                        ).then_inc(fk, 16)
                fv = p.inc("fet", 16 * 32)
                if p.on("gpsimd"):
                    for khc in range(8):
                        for ksub in range(4):
                            col = IDX_V + h * 32 + khc * 4 + ksub
                            p.q.indirect_dma_start(
                                out=vcache4[:, khc, ksub], out_offset=None,
                                in_=v_flat,
                                in_offset=bass.IndirectOffsetOnAxis(
                                    ap=idx3[:, col:col + 1], axis=0),
                            ).then_inc(fv, 16)
                fet_hkv = p.cnt["fet"]

                for qc in range(8):
                    fq = p.inc("fet", 16)
                    if p.on("gpsimd"):
                        if qc >= 2:
                            p.q.wait_ge(p.sems["mm"], qtile_mm_snap[(h, qc - 2)])
                        elif h >= 1:
                            p.q.wait_ge(p.sems["mm"], qtile_mm_snap[(h - 1, 6 + qc)])
                        p.q.indirect_dma_start(
                            out=t["qtile"][:, (qc % 2) * 512:(qc % 2) * 512 + 512], out_offset=None,
                            in_=q_flat,
                            in_offset=bass.IndirectOffsetOnAxis(
                                ap=idx3[:, IDX_K + h * 8 + qc:IDX_K + h * 8 + qc + 1], axis=0),
                        ).then_inc(fq, 16)
                    fet_q = p.cnt["fet"]

                    unit = 0
                    for khc in range(qc + 1):
                        for kt in range(4):
                            pS = t["pp"][unit % 2]
                            sm = p.inc("mm")
                            if p.on("tensor"):
                                p.q.wait_ge(p.sems["fet"], max(fet_hkv, fet_q))
                                if khc == 0 and kt == 0:
                                    p.q.wait_ge(p.sems["oev"], p.cnt["oev"])
                                if len(exp_att_snap) >= 2:
                                    p.q.wait_ge(p.sems["att"], exp_att_snap[-2])
                                p.q.matmul(pS[:, 0:512],
                                           t["kcache"][:, khc * 512 + kt * 128:khc * 512 + kt * 128 + 128],
                                           t["qtile"][:, (qc % 2) * 512:(qc % 2) * 512 + 512],
                                           start=True, stop=True).then_inc(sm, 1)
                            p.wait("scalar", "mm")
                            if p.on("scalar"):
                                if len(pexp_mm_snap) >= 2:
                                    p.q.wait_ge(p.sems["mm"], pexp_mm_snap[-2])
                                p.q.activation(out=t["ptile"][:, (unit % 2) * 512:(unit % 2) * 512 + 512],
                                               in_=pS[:, 0:512], func=AF.Exp)
                            p.fin("scalar", "att")
                            if khc == qc:
                                p.wait("vector", "att")
                                if p.on("vector"):
                                    p.q.tensor_tensor(
                                        out=t["ptile"][:, (unit % 2) * 512:(unit % 2) * 512 + 512],
                                        in0=t["ptile"][:, (unit % 2) * 512:(unit % 2) * 512 + 512],
                                        in1=t["tri"][:, kt * 512:(kt + 1) * 512], op=ALU.mult)
                                p.fin("vector", "att")
                            exp_att_snap.append(p.cnt["att"])
                            att_snap = p.cnt["att"]
                            sm2 = p.inc("mm", 2)
                            pexp_mm_snap.append(p.cnt["mm"])
                            if p.on("tensor"):
                                p.q.wait_ge(p.sems["att"], att_snap)
                                p.q.matmul(t["pp"][2][:, 0:512],
                                           vcache4[:, khc, kt],
                                           t["ptile"][:, (unit % 2) * 512:(unit % 2) * 512 + 512],
                                           start=(khc == 0 and kt == 0), stop=False)
                                p.q.matmul(t["prow"][:, 0:512], t["onesc"][:, :],
                                           t["ptile"][:, (unit % 2) * 512:(unit % 2) * 512 + 512],
                                           start=(khc == 0 and kt == 0),
                                           stop=(khc == qc and kt == 3)).then_inc(sm2, 2)
                            unit += 1
                    qtile_mm_snap[(h, qc)] = p.cnt["mm"]
                    p.wait("vector", "mm")
                    if p.on("vector"):
                        p.q.tensor_copy(t["rowf"][:, 0:512], t["prow"][:, 0:512])
                        p.q.drain()
                        p.q.reciprocal(t["rowf"][:, 0:512], t["rowf"][:, 0:512])
                    p.fin("vector", "att")
                    p.wait("scalar", "att")
                    if p.on("scalar"):
                        p.q.copy(t["rowb"][:, 0:512], t["rowf"][:, 0:512])
                    p.fin("scalar", "att")
                    sm3 = p.inc("mm")
                    if p.on("tensor"):
                        p.q.wait_ge(p.sems["att"], p.cnt["att"])
                        p.q.matmul(t["pp"][3][:, 0:512], t["onesr"][:, :], t["rowb"][:, 0:512],
                                   start=True, stop=True).then_inc(sm3, 1)
                    p.wait("scalar", "mm")
                    if p.on("scalar"):
                        p.q.copy(t["obuf"][:, 0:512], t["pp"][2][:, 0:512])
                    p.fin("scalar", "oev")
                    p.wait("vector", "oev")
                    if p.on("vector"):
                        p.q.tensor_tensor(out=t["obuf"][:, 512:1024], in0=t["obuf"][:, 0:512],
                                          in1=t["pp"][3][:, 0:512], op=ALU.mult)
                    p.fin("vector", "oev")
                    p.wait("sync", "oev")
                    so = p.inc("odma", 16)
                    if p.on("sync"):
                        p.q.dma_start(out=o_in[h * 128:(h + 1) * 128, qc * 512:(qc + 1) * 512],
                                      in_=t["obuf"][:, 512:1024]).then_inc(so, 16)
                    p.wait("scalar", "odma")
                head_mm_snap.append(p.cnt["mm"])

            odma_total = p.cnt["odma"]
            scc2 = p.inc("cc")
            if p.on("gpsimd"):
                p.q.wait_ge(p.sems["odma"], odma_total)
                p.q.collective_compute(
                    "AllGather", ALU.bypass, replica_groups=GROUPS,
                    ins=[o_in.ap().opt()], outs=[o_gat.ap().opt()]).then_inc(scc2, 1)
            cc4 = p.cnt["cc"]

            o_flat = o_gat.rearrange("f (qc c) -> (f qc) c", qc=8)
            ysb = lnxT
            wold_snap, woc_mm_snap, yev_snap = [], [], []
            for side in range(2):
                fa = p.inc("fet", 16 * 16)
                if p.on("gpsimd"):
                    if side == 0:
                        p.q.wait_ge(p.sems["cc"], cc4)
                    else:
                        p.q.wait_ge(p.sems["ymm"], p.cnt["ymm"])
                    for ft in range(NDT):
                        col = IDX_O + side * 16 + ft
                        p.q.indirect_dma_start(
                            out=t["aoT"][:, ft * 512:(ft + 1) * 512], out_offset=None,
                            in_=o_flat,
                            in_offset=bass.IndirectOffsetOnAxis(
                                ap=idx3[:, col:col + 1], axis=0),
                        ).then_inc(fa, 16)
                fet_side = p.cnt["fet"]
                if side == 0:
                    sb0 = p.inc("wold", 16)
                    if p.on("gpsimd"):
                        p.q.wait_ge(p.sems["mm"], kv_done_mm)
                        p.q.dma_start(out=ar[:, 16384:16384 + 2048],
                                      in_=bout_in[:, :]).then_inc(sb0, 16)
                for fc in range(4):
                    gi = side * 4 + fc
                    swo = p.inc("wold", 16)
                    wold_snap.append(p.cnt["wold"])
                    if p.on("gpsimd"):
                        if gi >= 2:
                            p.q.wait_ge(p.sems["ymm"], woc_mm_snap[gi - 2])
                        p.q.dma_start(
                            out=ar[:, (gi % 2) * 8192:(gi % 2) * 8192 + 8192].rearrange("p (dt c) -> p dt c", dt=NDT),
                            in_=wout.rearrange("(dt pp) c -> pp dt c", pp=128)[:, :, fc * 512:(fc + 1) * 512]).then_inc(swo, 16)
                    woc3 = ar[:, (gi % 2) * 8192:(gi % 2) * 8192 + 8192].rearrange("p (dt c) -> p dt c", dt=NDT)
                    for tl in range(4):
                        ts = side * 4 + tl
                        pr2 = gi * 4 + tl
                        pb = t["pp"][pr2 % 2]
                        sm = p.inc("ymm")
                        if p.on("tensor"):
                            p.q.wait_ge(p.sems["fet"], fet_side)
                            p.q.wait_ge(p.sems["wold"], wold_snap[-1])
                            if pr2 >= 2:
                                p.q.wait_ge(p.sems["yev"], yev_snap[pr2 - 2])
                            for ft in range(NDT):
                                ins = p.q.matmul(
                                    pb[:, 0:512],
                                    t["aoT"][:, ft * 512 + tl * 128:ft * 512 + tl * 128 + 128],
                                    woc3[:, ft],
                                    start=(ft == 0), stop=(ft == NDT - 1))
                                if ft == NDT - 1:
                                    ins.then_inc(sm, 1)
                        p.wait("vector", "ymm")
                        if p.on("vector"):
                            p.q.wait_ge(p.sems["wold"], 16)
                            p.q.tensor_tensor(out=ysb[:, ts * 2048 + fc * 512:ts * 2048 + fc * 512 + 512],
                                              in0=pb[:, 0:512],
                                              in1=ar[:, 16384 + fc * 512:16384 + fc * 512 + 512],
                                              op=ALU.add)
                        p.fin("vector", "yev")
                        yev_snap.append(p.cnt["yev"])
                    woc_mm_snap.append(p.cnt["ymm"])
            yev_total = p.cnt["yev"]

            xld2_snap = []
            res_snap = []
            sto_snap = []
            for ts in range(NTS):
                sx2 = p.inc("xld", 16)
                xld2_snap.append(p.cnt["xld"])
                if p.on("sync"):
                    p.q.wait_ge(p.sems["ln1"], ln1_snap[-1])
                    if ts >= 2:
                        p.q.wait_ge(p.sems["sto"], sto_snap[ts - 2])
                    p.q.dma_start(out=t["xbuf"][:, (ts % 2) * 2048:(ts % 2 + 1) * 2048],
                                  in_=x_in[ts * 128:(ts + 1) * 128, :]).then_inc(sx2, 16)

                def prew2(p2, ts=ts):
                    p2.wait("vector", "yev", yev_total)

                emit_ln_one(p, t,
                            ysb[:, ts * 2048:(ts + 1) * 2048],
                            ysb[:, ts * 2048:(ts + 1) * 2048],
                            8 + ts, "lny", pre_wait=prew2)
                if p.on("vector"):
                    p.q.wait_ge(p.sems["xld"], xld2_snap[ts])
                    p.q.tensor_tensor(out=t["xbuf"][:, (ts % 2) * 2048:(ts % 2 + 1) * 2048],
                                      in0=t["xbuf"][:, (ts % 2) * 2048:(ts % 2 + 1) * 2048],
                                      in1=ysb[:, ts * 2048:(ts + 1) * 2048],
                                      op=ALU.add)
                p.fin("vector", "res")
                res_snap.append(p.cnt["res"])
                ss2 = p.inc("sto", 16)
                sto_snap.append(p.cnt["sto"])
                if p.on("sync"):
                    p.q.wait_ge(p.sems["res"], res_snap[ts])
                    p.q.dma_start(out=x_out[ts * 128:(ts + 1) * 128, :],
                                  in_=t["xbuf"][:, (ts % 2) * 2048:(ts % 2 + 1) * 2048]).then_inc(ss2, 16)
            p.wait("sync", "sto")

        for role in ROLES:
            getattr(block, role)(lambda q, role=role: program(P(role, q, sems)))

    return nc



def core_slices():
    return [(c // 4, c % 4, 7 - c % 4) for c in range(8)]


def pack_x(x):
    return [np.ascontiguousarray(
        np.concatenate([x[b, a * HC:(a + 1) * HC], x[b, bb * HC:(bb + 1) * HC]], axis=0))
        for b, a, bb in core_slices()]


def unpack_x(shards, dtype=np.float32):
    x = np.empty((B, S, D), dtype)
    for c, (b, a, bb) in enumerate(core_slices()):
        x[b, a * HC:(a + 1) * HC] = shards[c][:HC]
        x[b, bb * HC:(bb + 1) * HC] = shards[c][HC:]
    return x


def carry_masks():
    out = []
    for b, a, bb in core_slices():
        m = np.zeros((128, 16), np.float32)
        for side, h in ((0, a), (1, bb)):
            m[:, side * 8:side * 8 + h] = 1.0
        out.append(m)
    return out


_IDENT = np.eye(128, dtype=np.float32).astype(BF)


def _rope_cs():
    half = 32
    theta = 1.0 / 10000.0 ** (np.arange(0, DHD, 2, dtype=np.float32) / DHD)
    idx = np.arange(S, dtype=np.float32)[:, None] * theta[None, :]
    return (np.tile(np.cos(idx[:, :half]), (1, 2)) + np.tile(np.sin(idx[:, :half]), (1, 2))).astype(np.float32)


def l1_attn_aux(core):
    b, a, bb = core_slices()[core]
    j = core % 4
    H0 = 4 * j
    cs = _rope_cs()
    toks = np.concatenate([np.arange(a * HC, (a + 1) * HC), np.arange(bb * HC, (bb + 1) * HC)])
    csT = np.empty((128, TOK), np.float32)
    for p_ in range(128):
        csT[p_, :] = cs[toks, p_ % 64]
    idx = np.zeros((128, 192), np.uint32)
    pp = np.arange(128, dtype=np.uint32)
    for h in range(4):
        H = H0 + h
        for ch in range(8):
            r, sd = hc_owner(ch)
            idx[:, 0 + h * 8 + ch] = (r * 2048 + H * 128 + pp) * 2 + sd
            for ksub in range(4):
                idx[:, 32 + h * 32 + ch * 4 + ksub] = (r * 1024 + sd * 512 + ksub * 128 + pp) * 16 + H
    for side, hc in ((0, a), (1, bb)):
        for ft in range(NDT):
            idx[:, 160 + side * 16 + ft] = (ft * 128 + pp) * 8 + hc
    return csT, idx


_TRI = None


def _tri_tiles():
    global _TRI
    if _TRI is None:
        m = np.zeros((128, 4 * 512), np.float32)
        for kt in range(4):
            for k_ in range(128):
                m[k_, kt * 512:(kt + 1) * 512] = (np.arange(512) >= kt * 128 + k_)
        _TRI = m.astype(BF)
    return _TRI


def run_l1_attn(x, W):
    shards = pack_x(x)
    nc = build_l1_attn()
    gq_t = np.ascontiguousarray(W["g_q"].reshape(NCT, 128).T)
    gkv_t = np.ascontiguousarray(W["g_kv"].reshape(NCT, 128).T)
    bout = np.ascontiguousarray(np.broadcast_to(W["b_out"][None, :], (128, D)))
    onesc = np.ones((128, 1), np.float32).astype(BF)
    onesr = np.ones((1, 128), np.float32).astype(BF)
    in_maps = []
    for c in range(8):
        csT, idx = l1_attn_aux(c)
        in_maps.append({
            "x_hc": shards[c], "wdq": W["W_dq"], "wdkv": W["W_dkv"], "wuq": W["W_uq"],
            "wqr": W["W_qr"], "wuk": W["W_uk"], "wuv": W["W_uv"], "wkr": W["W_kr"],
            "wout": W["W_out"], "gq_hbm": gq_t, "gkv_hbm": gkv_t, "bout_hbm": bout,
            "cs_hbm": csT, "tri_hbm": _tri_tiles(), "idx_hbm": idx,
            "onesc_hbm": onesc, "onesr_hbm": onesr, "ident_hbm": _IDENT,
        })
    res = run_bass_kernel_spmd(nc, in_maps, core_ids=list(range(8)))
    return unpack_x([res.results[c]["x3_hc"] for c in range(8)])


def kernel(**inputs):
    i = inputs
    x1 = run_l0_attn(i["L0_attn_x"].astype(np.float32), i["L0_attn_Wq"], i["L0_attn_Wk"],
                     i["L0_attn_Wv"], i["L0_attn_Wo"])
    x2 = run_ffn(x1,
                 np.ascontiguousarray(i["L0_ffn_gate_w"].T),
                 np.ascontiguousarray(i["L0_ffn_up_w"].T),
                 np.ascontiguousarray(i["L0_ffn_down_w"].T),
                 True, i["L0_ffn_gate_scale"], i["L0_ffn_up_scale"], i["L0_ffn_down_scale"])
    W = {k[8:]: v for k, v in i.items() if k.startswith("L1_attn_")}
    x3 = run_l1_attn(x2, W)
    x4 = run_ffn(x3,
                 np.ascontiguousarray(i["L1_ffn_w_gate"].T),
                 np.ascontiguousarray(i["L1_ffn_w_up"].T),
                 np.ascontiguousarray(i["L1_ffn_w_down"].T),
                 False)
    return x4
